# version 21
# baseline (speedup 1.0000x reference)
import numpy as np
import contextlib

import concourse.bass as bass
import concourse.mybir as mybir
from concourse.bass_utils import run_bass_kernel_spmd

F32 = mybir.dt.float32
BF16 = mybir.dt.bfloat16
AF = mybir.ActivationFunctionType
ALU = mybir.AluOpType
AX = mybir.AxisListType

TOK = 2048
D = 1024
HID = 4096
L = 2048
NT = TOK // 128
NCH = 32
WIN = 17
RING = 18
NQ = 4
TQ = TOK // NQ
NTQ = TQ // 128

ENGINES = ("tensor", "vector", "scalar", "sync", "gpsimd")


class Sem:
    __slots__ = ("h", "n", "name")

    def __init__(self, es, nc, name):
        self.h = es.enter_context(nc.semaphore(name))
        self.n = 0
        self.name = name


class Sched:
    def __init__(self):
        self.ops = {e: [] for e in ENGINES}
        self.lastwait = {e: {} for e in ENGINES}

    def wait(self, e, ev):
        if ev is None:
            return
        sem, v = ev
        if v <= 0:
            return
        lw = self.lastwait[e]
        if lw.get(sem, 0) >= v:
            return
        lw[sem] = v
        self.ops[e].append(lambda eng, s=sem, vv=v: eng.wait_ge(s.h, vv))

    def waits(self, e, evs):
        for ev in evs:
            self.wait(e, ev)

    def comp(self, e, fn, inc=None, k=1):
        self.ops[e].append(lambda eng: eng.drain())
        return self.run(e, fn, inc, k)

    def run(self, e, fn, inc=None, k=1):
        if inc is not None:
            inc.n += k
            v = inc.n
            self.ops[e].append(lambda eng, f=fn, s=inc, kk=k: f(eng).then_inc(s.h, kk))
            return (inc, v)
        self.ops[e].append(fn)
        return None


class Slot:
    __slots__ = ("w", "r")

    def __init__(self):
        self.w = None
        self.r = []


def build_nc(dbg=0):
    nc = bass.Bass("TRN2", target_bir_lowering=False)

    x_h = nc.dram_tensor("x", [33 * 128, D], F32, kind="ExternalInput").ap()
    F_h = nc.dram_tensor("Ftp", [WIN, 128, 128], F32, kind="ExternalInput").ap()
    w1_h = nc.dram_tensor("w1", [D, HID], F32, kind="ExternalInput").ap()
    w2_h = nc.dram_tensor("w2", [D, HID], F32, kind="ExternalInput").ap()
    h1_h = nc.dram_tensor("h1", [HID, HID], F32, kind="ExternalInput").ap()
    h2_h = nc.dram_tensor("h2", [HID, HID], F32, kind="ExternalInput").ap()
    w3_h = nc.dram_tensor("w3", [HID, D], F32, kind="ExternalInput").ap()
    s1_h = nc.dram_tensor("s1r", [128, HID // 128], F32, kind="ExternalInput").ap()
    s2_h = nc.dram_tensor("s2r", [128, HID // 128], F32, kind="ExternalInput").ap()
    sh1_h = nc.dram_tensor("sh1r", [128, HID // 128], F32, kind="ExternalInput").ap()
    sh2_h = nc.dram_tensor("sh2r", [128, HID // 128], F32, kind="ExternalInput").ap()
    s3b_h = nc.dram_tensor("s3b", [128, D], F32, kind="ExternalInput").ap()
    out_h = nc.dram_tensor("out", [TOK, D], F32, kind="ExternalOutput").ap()

    w1s = nc.dram_tensor("w1s", [HID // 512, 128, D // 128, 512], BF16, kind="Internal").ap()
    w2s = nc.dram_tensor("w2s", [HID // 512, 128, D // 128, 512], BF16, kind="Internal").ap()
    h1s = nc.dram_tensor("h1s", [HID // 256, 128, HID // 128, 256], BF16, kind="Internal").ap()
    h2s = nc.dram_tensor("h2s", [HID // 256, 128, HID // 128, 256], BF16, kind="Internal").ap()
    w3s = nc.dram_tensor("w3s", [D // 256, 128, HID // 128, 256], BF16, kind="Internal").ap()
    x1bf = [nc.dram_tensor(f"x1bf{i}", [TOK, 128], BF16, kind="Internal").ap() for i in range(D // 128)]
    if dbg >= 1:
        x1f_h = nc.dram_tensor("x1f", [TOK, D], F32, kind="ExternalOutput").ap()
    else:
        x1f_h = nc.dram_tensor("x1f", [TOK, D], F32, kind="Internal").ap()
    dbg_g = nc.dram_tensor("dbg_g", [128, HID // 128 * TQ], BF16, kind="ExternalOutput").ap() if dbg >= 2 else None
    dbg_u = nc.dram_tensor("dbg_u", [128, HID // 128 * TQ], BF16, kind="ExternalOutput").ap() if dbg >= 3 else None
    dbg_v = nc.dram_tensor("dbg_v", [128, HID // 128 * TQ], BF16, kind="ExternalOutput").ap() if dbg >= 4 else None

    es = contextlib.ExitStack()
    sc = Sched()

    S = lambda name: Sem(es, nc, name)
    s_ld = S("s_ld")
    s_xst = [S(f"s_xst{i}") for i in range(2)]
    s_xres = [S(f"s_xres{i}") for i in range(2)]
    s_x1r = [S(f"s_x1r{i}") for i in range(2)]
    s_wld = [S(f"s_wld{i}") for i in range(5)]
    s_wwr = [S(f"s_wwr{i}") for i in range(5)]
    s_tx1 = S("s_tx1")
    s_tw1 = [S(f"s_tw1_{i}") for i in range(2)]
    s_tw2 = [S(f"s_tw2_{i}") for i in range(2)]
    s_th = [S(f"s_th{i}") for i in range(2)]
    s_tw3 = S("s_tw3")
    s_x1st = [S(f"s_x1st{i}") for i in range(2)]
    s_x1bst = [S(f"s_x1bst{i}") for i in range(2)]
    s_out = [S(f"s_out{i}") for i in range(4)]
    s_dbg = S("s_dbg")
    s_pe = S("s_pe")
    s_ve = S("s_ve")
    s_sc = S("s_sc")

    sb = lambda name, shape, dt: es.enter_context(nc.sbuf_tensor(name, shape, dt))
    cpool = sb("cpool", [128, 4], F32)
    sc_s1 = sb("sc_s1", [128, 32], F32)
    sc_s2 = sb("sc_s2", [128, 32], F32)
    sc_h1 = sb("sc_h1", [128, 32], F32)
    sc_h2 = sb("sc_h2", [128, 32], F32)
    s3bt = sb("s3bt", [128, D], F32)
    stats = sb("stats", [128, 16], F32)
    sqs = sb("sqs", [128, D // 2], F32)
    x1T = sb("x1T", [128, (D // 128) * TQ], BF16)
    wsl1 = [sb(f"wsl1_{i}", [128, (D // 128) * 512], BF16) for i in range(2)]
    wsl2 = [sb(f"wsl2_{i}", [128, (D // 128) * 512], BF16) for i in range(2)]
    hsl = [sb(f"hsl{i}", [128, (HID // 128) * 256], BF16) for i in range(2)]
    tmpf = [sb(f"tmpf{i}", [128, D], F32) for i in range(2)]
    wstgf = [sb(f"wstgf{i}", [128, 512], F32) for i in range(5)]
    wstgb = [sb(f"wstgb{i}", [128, 512], BF16) for i in range(5)]

    ps = [es.enter_context(nc.psum_tensor(f"ps{i}", [128, 512], F32)) for i in range(8)]
    bank_evac = [None] * 8
    bank_group = [None] * 8

    def pe_group(bank, mms, width=512):
        sc.wait("tensor", bank_evac[bank])
        n = len(mms)
        out_ap = ps[bank][:] if width == 512 else ps[bank][:, 0:width]
        for i, (lt, rh) in enumerate(mms):
            st = (i == 0)
            sp = (i == n - 1)
            if sp:
                ev = sc.run("tensor", lambda t, a=lt, b=rh, o=out_ap, s=st: t.matmul(o, a, b, start=s, stop=True), inc=s_pe)
            else:
                sc.run("tensor", lambda t, a=lt, b=rh, o=out_ap, s=st: t.matmul(o, a, b, start=s, stop=False))
        bank_group[bank] = ev
        return ev

    es_conv = contextlib.ExitStack()
    cb = lambda name, shape, dt: es_conv.enter_context(nc.sbuf_tensor(name, shape, dt))
    xwh = [cb(f"xwh{i}", [128, D], BF16) for i in range(RING)]
    xwl = [cb(f"xwl{i}", [128, D], BF16) for i in range(RING)]
    xstg = [cb(f"xstg{i}", [128, D], F32) for i in range(2)]
    xres = [cb(f"xres{i}", [128, D], F32) for i in range(2)]
    x1ft = [cb(f"x1ft{i}", [128, D], F32) for i in range(2)]
    Fh = cb("Fh", [128, WIN * 128], BF16)
    Fl = cb("Fl", [128, WIN * 128], BF16)

    sc.run("gpsimd", lambda g: g.dma_start(sc_s1[:], s1_h), inc=s_ld, k=16)
    sc.run("gpsimd", lambda g: g.dma_start(sc_s2[:], s2_h), inc=s_ld, k=16)
    sc.run("gpsimd", lambda g: g.dma_start(sc_h1[:], sh1_h), inc=s_ld, k=16)
    sc.run("gpsimd", lambda g: g.dma_start(sc_h2[:], sh2_h), inc=s_ld, k=16)
    sc.run("gpsimd", lambda g: g.dma_start(s3bt[:], s3b_h), inc=s_ld, k=16)
    ld_total = (s_ld, s_ld.n)

    wstgf_slot = [Slot() for _ in range(5)]
    wstgb_slot = [Slot() for _ in range(5)]

    sc.comp("vector", lambda v: v.memset(cpool[:, 0:1], -0.5))
    sc.comp("vector", lambda v: v.memset(cpool[:, 1:2], 1e-5))
    sc.comp("vector", lambda v: v.memset(cpool[:, 2:3], 1.0 / D))
    sc.wait("vector", ld_total)
    fprep = None
    for c in range(WIN):
        sl = c % 5
        sc.waits("sync", wstgf_slot[sl].r)
        e_l = sc.run("sync", lambda g, c=c, sl=sl: g.dma_start(wstgf[sl][:, 0:128], F_h[c, :, :]), inc=s_wld[sl], k=16)
        wstgf_slot[sl].w = e_l
        wstgf_slot[sl].r = []
        sc.wait("vector", e_l)
        sc.comp("vector", lambda v, c=c, sl=sl: v.tensor_copy(Fh[:, c * 128:(c + 1) * 128], wstgf[sl][:, 0:128]))
        fprep = sc.comp("vector", lambda v, c=c, sl=sl: v.tensor_sub(Fl[:, c * 128:(c + 1) * 128], wstgf[sl][:, 0:128], Fh[:, c * 128:(c + 1) * 128]), inc=s_ve)
        wstgf_slot[sl].r = [fprep]

    cast_jobs = []
    cast_range = {}
    def cast_weight(name, srch, dst, I, O, obw, wq=None):
        i0 = len(cast_jobs)
        nic = I // 128
        for c0 in range(0, O, 512):
            w = min(512, O - c0)
            for ic in range(nic):
                cast_jobs.append((name, srch, dst, ic, c0, w, obw, ic == nic - 1, wq))
        cast_range[name] = (i0, len(cast_jobs))
    cast_weight("w1", w1_h, w1s, D, HID, 512)
    cast_weight("w2", w2_h, w2s, D, HID, 512)
    cast_weight("h1", h1_h, h1s, HID, HID, 256)
    cast_weight("h2", h2_h, h2s, HID, HID, 256, wq="scalar")
    cast_weight("w3", w3_h, w3s, HID, D, 256, wq="scalar")
    cast_state = {"i": 0, "wq": 0}
    cast_grp = {}
    def emit_casts(k):
        for _ in range(k):
            i = cast_state["i"]
            if i >= len(cast_jobs):
                return
            cast_state["i"] += 1
            name, srch, dst, ic, c0, w, obw, grp_end, wq = cast_jobs[i]
            sl = i % 5
            sc.waits("sync", wstgf_slot[sl].r)
            e_l = sc.run("sync", lambda g, srch=srch, ic=ic, c0=c0, w=w, sl=sl:
                         g.dma_start(wstgf[sl][:, 0:w], srch[ic * 128:(ic + 1) * 128, c0:c0 + w]), inc=s_wld[sl], k=16)
            wstgf_slot[sl].w = e_l
            wstgf_slot[sl].r = []
            sc.wait("vector", e_l)
            sc.waits("vector", wstgb_slot[sl].r)
            e_c = sc.comp("vector", lambda v, w=w, sl=sl: v.tensor_copy(wstgb[sl][:, 0:w], wstgf[sl][:, 0:w]), inc=s_ve)
            wstgf_slot[sl].r = [e_c]
            wstgb_slot[sl].w = e_c
            wstgb_slot[sl].r = []
            last = None
            for j in range(w // obw):
                ob = c0 // obw + j
                if wq is None:
                    qe = "sync" if (cast_state["wq"] % 2 == 0) else "scalar"
                    cast_state["wq"] += 1
                else:
                    qe = wq
                sc.wait(qe, e_c)
                last = sc.run(qe, lambda s, dst=dst, ob=ob, ic=ic, sl=sl, j=j, obw=obw:
                              s.dma_start(dst[ob, :, ic, :], wstgb[sl][:, j * obw:(j + 1) * obw]), inc=s_wwr[sl], k=16)
            wstgb_slot[sl].r = [last]
            if grp_end:
                cast_grp[(name, c0)] = [(sw, sw.n) for sw in s_wwr]
    def emit_casts_until(name):
        emit_casts(cast_range[name][1] - cast_state["i"])
    def cast_grp_ev(name, ob, obw):
        return cast_grp[(name, (ob * obw // 512) * 512)]

    xst_slot = [Slot() for _ in range(2)]
    ring_h = [Slot() for _ in range(RING)]
    ring_l = [Slot() for _ in range(RING)]
    xres_slot = [Slot() for _ in range(2)]
    x1ft_slot = [Slot() for _ in range(2)]
    stats_slot = [Slot() for _ in range(2)]

    def load_chunk(w):
        sl = w % 2
        rs = w % RING
        sc.waits("gpsimd", xst_slot[sl].r)
        ev = sc.run("gpsimd", lambda g, w=w, sl=sl: g.dma_start(xstg[sl][:], x_h[w * 128:(w + 1) * 128, :]), inc=s_xst[sl], k=16)
        xst_slot[sl].w = ev
        xst_slot[sl].r = []
        sc.wait("vector", ev)
        sc.waits("vector", ring_h[rs].r)
        sc.waits("vector", ring_l[rs].r)
        e1 = sc.comp("vector", lambda v, rs=rs, sl=sl: v.tensor_copy(xwh[rs][:], xstg[sl][:]), inc=s_ve)
        e2 = sc.comp("vector", lambda v, rs=rs, sl=sl: v.tensor_sub(xwl[rs][:], xstg[sl][:], xwh[rs][:]), inc=s_ve)
        xst_slot[sl].r.extend([e1, e2])
        ring_h[rs].w = e1
        ring_h[rs].r = []
        ring_l[rs].w = e2
        ring_l[rs].r = []

    def ln_chain(engine_y_ev, ytile, dst_f32, tile_idx):
        st = tile_idx % 2
        so = st * 8
        sc.waits("vector", stats_slot[st].r)
        stats_slot[st].r = []
        sc.wait("vector", engine_y_ev)
        sc.comp("vector", lambda v: v.tensor_reduce(stats[:, so:so + 1], ytile[:], AX.X, ALU.add))
        sc.comp("vector", lambda v: v.tensor_tensor(sqs[:], ytile[:, 0:D // 2], ytile[:, 0:D // 2], ALU.mult))
        sc.comp("vector", lambda v: v.tensor_reduce(stats[:, so + 1:so + 2], sqs[:], AX.X, ALU.add))
        sc.comp("vector", lambda v: v.tensor_tensor(sqs[:], ytile[:, D // 2:D], ytile[:, D // 2:D], ALU.mult))
        sc.comp("vector", lambda v: v.tensor_reduce(stats[:, so + 2:so + 3], sqs[:], AX.X, ALU.add))
        sc.comp("vector", lambda v: v.tensor_add(stats[:, so + 1:so + 2], stats[:, so + 1:so + 2], stats[:, so + 2:so + 3]))
        sc.comp("vector", lambda v: v.tensor_scalar(stats[:, so + 2:so + 3], stats[:, so:so + 1], cpool[:, 2:3], None, ALU.mult))
        sc.comp("vector", lambda v: v.tensor_scalar(stats[:, so + 3:so + 4], stats[:, so + 1:so + 2], cpool[:, 2:3], cpool[:, 1:2], ALU.mult, ALU.add))
        sc.comp("vector", lambda v: v.tensor_tensor(stats[:, so + 4:so + 5], stats[:, so + 2:so + 3], stats[:, so + 2:so + 3], ALU.mult))
        e_v2 = sc.comp("vector", lambda v: v.tensor_sub(stats[:, so + 5:so + 6], stats[:, so + 3:so + 4], stats[:, so + 4:so + 5]), inc=s_ve)
        sc.wait("scalar", e_v2)
        sc.comp("scalar", lambda s: s.activation(stats[:, so + 6:so + 7], stats[:, so + 5:so + 6], AF.Ln))
        e_rs = sc.comp("scalar", lambda s: s.activation(stats[:, so + 7:so + 8], stats[:, so + 6:so + 7], AF.Exp, scale=cpool[:, 0:1]), inc=s_sc)
        sc.wait("vector", e_rs)
        e_n = sc.comp("vector", lambda v: v.tensor_scalar(dst_f32[:], ytile[:], stats[:, so + 2:so + 3], stats[:, so + 7:so + 8], ALU.subtract, ALU.mult), inc=s_ve)
        stats_slot[st].r = [e_n]
        return e_n

    for w in range(WIN):
        load_chunk(w)
        emit_casts(1)

    for ti in range(NT):
        if ti + WIN < NCH:
            load_chunk(ti + WIN)
        rsl = ti % 2
        sc.waits("gpsimd", xres_slot[rsl].r)
        e_res = sc.run("gpsimd", lambda g, ti=ti, rsl=rsl: g.dma_start(xres[rsl][:], x_h[ti * 128:(ti + 1) * 128, :]), inc=s_xres[rsl], k=16)
        xres_slot[rsl].w = e_res
        xres_slot[rsl].r = []
        emit_casts(32)

        sc.wait("tensor", fprep)
        for cc in range(WIN):
            sc.wait("tensor", ring_h[(ti + cc) % RING].w)
            sc.wait("tensor", ring_l[(ti + cc) % RING].w)
        grp_ev = [None, None]
        for g in range(2):
            bank = (ti % 2) * 2 + g
            mms = []
            for cc in range(WIN):
                rs = (ti + cc) % RING
                mms.append((Fh[:, cc * 128:(cc + 1) * 128], xwh[rs][:, g * 512:(g + 1) * 512]))
                mms.append((Fh[:, cc * 128:(cc + 1) * 128], xwl[rs][:, g * 512:(g + 1) * 512]))
            for cc in range(WIN):
                rs = (ti + cc) % RING
                mms.append((Fl[:, cc * 128:(cc + 1) * 128], xwh[rs][:, g * 512:(g + 1) * 512]))
            grp_ev[g] = pe_group(bank, mms)
        for cc in range(WIN):
            rs = (ti + cc) % RING
            ring_h[rs].r = [grp_ev[1]]
            ring_l[rs].r = [grp_ev[1]]

        fsl = ti % 2
        sc.waits("vector", x1ft_slot[fsl].r)
        x1ft_slot[fsl].r = []
        sc.wait("vector", e_res)
        ev_y = None
        for g in range(2):
            bank = (ti % 2) * 2 + g
            sc.wait("vector", grp_ev[g])
            ev_y = sc.comp("vector", lambda v, b=bank, g=g, fsl=fsl, rr=rsl: v.tensor_add(x1ft[fsl][:, g * 512:(g + 1) * 512], ps[b][:], xres[rr][:, g * 512:(g + 1) * 512]), inc=s_ve)
            bank_evac[bank] = ev_y
        xres_slot[rsl].r = [ev_y]
        e_n = ln_chain(ev_y, x1ft[fsl], x1ft[fsl], ti)
        x1ft_slot[fsl].w = e_n

        sc.wait("scalar", e_n)
        e_st = sc.run("scalar", lambda s, ti=ti, fsl=fsl: s.dma_start(x1f_h[ti * 128:(ti + 1) * 128, :], x1ft[fsl][:]), inc=s_x1st[fsl], k=16)
        x1ft_slot[fsl].r.append(e_st)
        sc.wait("gpsimd", e_n)
        last_b = None
        for ic in range(D // 128):
            last_b = sc.run("gpsimd", lambda g, ti=ti, ic=ic, fsl=fsl: g.dma_start(x1bf[ic][ti * 128:(ti + 1) * 128, :], x1ft[fsl][:, ic * 128:(ic + 1) * 128]), inc=s_x1bst[fsl], k=16)
        x1ft_slot[fsl].r.append(last_b)

    conv_pe_done = bank_group[:]

    x1T_slot = Slot()
    wsl1_slot = [Slot() for _ in range(2)]
    wsl2_slot = [Slot() for _ in range(2)]
    hsl_slot = [Slot() for _ in range(2)]
    tmpf_slot = [Slot() for _ in range(2)]
    y2_slot = [Slot() for _ in range(4)]

    NOG = HID // 128

    es_conv.close()
    gT = sb("gT", [128, NOG * TQ], BF16)
    uT = sb("uT", [128, NOG * TQ], BF16)
    y2 = [sb(f"y2_{i}", [128, D], F32) for i in range(4)]
    gT_w = [None] * NOG
    uT_w = [None] * NOG
    gT_rd = None
    uT_rd = None

    def dump_sbuf(tag, src_ap, dst_h, ready_evs):
        sc.waits("scalar", ready_evs)
        sc.run("scalar", lambda s: s.dma_start(dst_h, src_ap[:]), inc=s_dbg, k=16)
        sc.wait("scalar", (s_dbg, s_dbg.n))
        sc.wait("vector", (s_dbg, s_dbg.n))
        sc.wait("tensor", (s_dbg, s_dbg.n))

    emit_casts_until("w2")

    for q in range(NQ):
        sc.waits("sync", x1T_slot.r)
        x1T_slot.r = []
        sc.wait("sync", (s_x1bst[0], s_x1bst[0].n))
        sc.wait("sync", (s_x1bst[1], s_x1bst[1].n))
        for ic in range(D // 128):
            e = sc.run("sync", lambda s, ic=ic, q=q: s.dma_start(x1T[:, ic * TQ:(ic + 1) * TQ], x1bf[ic][q * TQ:(q + 1) * TQ, :], transpose=True), inc=s_tx1, k=16)
        x1T_slot.w = (s_tx1, s_tx1.n)

        for ob in range(HID // 512):
            sl = ob % 2
            sc.waits("sync", cast_grp_ev("w1", ob, 512))
            sc.waits("sync", cast_grp_ev("w2", ob, 512))
            sc.waits("sync", wsl1_slot[sl].r)
            wsl1_slot[sl].r = []
            sc.run("sync", lambda s, ob=ob, sl=sl: s.dma_start(wsl1[sl][:], w1s[ob, :, :, :]), inc=s_tw1[sl], k=16)
            wsl1_slot[sl].w = (s_tw1[sl], s_tw1[sl].n)
            sc.waits("sync", wsl2_slot[sl].r)
            wsl2_slot[sl].r = []
            sc.run("sync", lambda s, ob=ob, sl=sl: s.dma_start(wsl2[sl][:], w2s[ob, :, :, :]), inc=s_tw2[sl], k=16)
            wsl2_slot[sl].w = (s_tw2[sl], s_tw2[sl].n)

            sc.wait("tensor", x1T_slot.w)
            sc.wait("tensor", wsl1_slot[sl].w)
            sc.wait("tensor", wsl2_slot[sl].w)
            for ogl in range(4):
                og = ob * 4 + ogl
                b1 = (og % 2) * 4
                b2 = b1 + 1
                mm1 = [(wsl1[sl][:, ic * 512 + ogl * 128: ic * 512 + ogl * 128 + 128], x1T[:, ic * TQ:(ic + 1) * TQ]) for ic in range(D // 128)]
                mm2 = [(wsl2[sl][:, ic * 512 + ogl * 128: ic * 512 + ogl * 128 + 128], x1T[:, ic * TQ:(ic + 1) * TQ]) for ic in range(D // 128)]
                e1 = pe_group(b1, mm1)
                e2 = pe_group(b2, mm2)
                wsl1_slot[sl].r = [e1]
                wsl2_slot[sl].r = [e2]
                x1T_slot.r = [e2]
                tsl = og % 2
                sc.waits("scalar", tmpf_slot[tsl].r)
                tmpf_slot[tsl].r = []
                sc.wait("scalar", e1)
                e_sil = sc.comp("scalar", lambda s, b=b1, og=og, tsl=tsl: s.activation(tmpf[tsl][:, 0:512], ps[b][:], AF.Silu, scale=sc_s1[:, og:og + 1]), inc=s_sc)
                bank_evac[b1] = e_sil
                tmpf_slot[tsl].w = e_sil
                sc.wait("vector", e2)
                sc.wait("vector", e_sil)
                e_g = sc.comp("vector", lambda v, b=b2, og=og, tsl=tsl: v.scalar_tensor_tensor(gT[:, og * TQ:(og + 1) * TQ], ps[b][:], sc_s2[:, og:og + 1], tmpf[tsl][:, 0:512], ALU.mult, ALU.mult), inc=s_ve)
                bank_evac[b2] = e_g
                tmpf_slot[tsl].r = [e_g]
                gT_w[og] = e_g
        if dbg >= 2 and q == 0:
            dump_sbuf("g", gT, dbg_g, list(gT_w))

        if q == 0:
            emit_casts_until("h1")
        uT_w_new = [None] * NOG
        for ob in range(HID // 256):
            sl = ob % 2
            sc.waits("sync", cast_grp_ev("h1", ob, 256))
            sc.waits("sync", hsl_slot[sl].r)
            hsl_slot[sl].r = []
            sc.run("sync", lambda s, ob=ob, sl=sl: s.dma_start(hsl[sl][:], h1s[ob, :, :, :]), inc=s_th[sl], k=16)
            hsl_slot[sl].w = (s_th[sl], s_th[sl].n)
            sc.wait("tensor", hsl_slot[sl].w)
            last_e = None
            for ogl in range(2):
                og = ob * 2 + ogl
                bank = (2, 3, 6, 7)[og % 4]
                sc.waits("tensor", [gT_w[ic] for ic in range(NOG)])
                mms = [(hsl[sl][:, ic * 256 + ogl * 128: ic * 256 + ogl * 128 + 128], gT[:, ic * TQ:(ic + 1) * TQ]) for ic in range(NOG)]
                e = pe_group(bank, mms)
                last_e = e
                sc.wait("scalar", e)
                e_u = sc.comp("scalar", lambda s, b=bank, og=og: s.activation(uT[:, og * TQ:(og + 1) * TQ], ps[b][:], AF.Gelu, scale=sc_h1[:, og:og + 1]), inc=s_sc)
                bank_evac[bank] = e_u
                uT_w_new[og] = e_u
            hsl_slot[sl].r = [last_e]
            if q == 0:
                emit_casts(16)
        uT_w = uT_w_new
        gT_rd = (s_pe, s_pe.n)
        if dbg >= 3 and q == 0:
            dump_sbuf("u", uT, dbg_u, list(uT_w))

        for ob in range(HID // 256):
            sl = ob % 2
            sc.waits("sync", cast_grp_ev("h2", ob, 256))
            sc.waits("sync", hsl_slot[sl].r)
            hsl_slot[sl].r = []
            sc.run("sync", lambda s, ob=ob, sl=sl: s.dma_start(hsl[sl][:], h2s[ob, :, :, :]), inc=s_th[sl], k=16)
            hsl_slot[sl].w = (s_th[sl], s_th[sl].n)
            sc.wait("tensor", hsl_slot[sl].w)
            last_e = None
            for ogl in range(2):
                og = ob * 2 + ogl
                bank = (2, 3, 6, 7)[og % 4]
                sc.waits("tensor", [uT_w[ic] for ic in range(NOG)])
                sc.wait("tensor", gT_rd)
                mms = [(hsl[sl][:, ic * 256 + ogl * 128: ic * 256 + ogl * 128 + 128], uT[:, ic * TQ:(ic + 1) * TQ]) for ic in range(NOG)]
                e = pe_group(bank, mms)
                last_e = e
                sc.wait("scalar", e)
                e_v = sc.comp("scalar", lambda s, b=bank, og=og: s.activation(gT[:, og * TQ:(og + 1) * TQ], ps[b][:], AF.Gelu, scale=sc_h2[:, og:og + 1]), inc=s_sc)
                bank_evac[bank] = e_v
                gT_w[og] = e_v
            hsl_slot[sl].r = [last_e]
            if q == 0:
                emit_casts(4)
        uT_rd = (s_pe, s_pe.n)
        if dbg >= 4 and q == 0:
            dump_sbuf("v", gT, dbg_v, list(gT_w))

        y2_evs = [[None] * (D // 256) for _ in range(NTQ)]
        for ob in range(D // 256):
            sl = ob % 2
            sc.waits("sync", cast_grp_ev("w3", ob, 256))
            sc.waits("sync", hsl_slot[sl].r)
            hsl_slot[sl].r = []
            sc.run("sync", lambda s, ob=ob, sl=sl: s.dma_start(hsl[sl][:], w3s[ob, :, :, :]), inc=s_th[sl], k=16)
            hsl_slot[sl].w = (s_th[sl], s_th[sl].n)
            sc.wait("tensor", hsl_slot[sl].w)
            last_e = None
            for tl in range(NTQ):
                bank = (tl % 4) * 2 + (ob % 2)
                sc.waits("tensor", [gT_w[icx] for icx in range(NOG)])
                mms = [(gT[:, ic * TQ + tl * 128: ic * TQ + tl * 128 + 128], hsl[sl][:, ic * 256:(ic + 1) * 256]) for ic in range(NOG)]
                e = pe_group(bank, mms, width=256)
                last_e = e
                ysl = tl
                if ob == 0:
                    sc.waits("vector", y2_slot[ysl].r)
                    y2_slot[ysl].r = []
                sc.wait("vector", e)
                e_c = sc.comp("vector", lambda v, b=bank, ob=ob, ysl=ysl: v.tensor_mul(y2[ysl][:, ob * 256:(ob + 1) * 256], ps[b][:, 0:256], s3bt[:, ob * 256:(ob + 1) * 256]), inc=s_ve)
                bank_evac[bank] = e_c
                y2_evs[tl][ob] = e_c
            hsl_slot[sl].r = [last_e]
        for tl in range(NTQ):
            ti = q * NTQ + tl
            rsl = tl % 2
            sc.waits("gpsimd", tmpf_slot[rsl].r)
            tmpf_slot[rsl].r = []
            sc.wait("gpsimd", (s_x1st[0], s_x1st[0].n))
            sc.wait("gpsimd", (s_x1st[1], s_x1st[1].n))
            e_xr = sc.run("gpsimd", lambda g, ti=ti, rsl=rsl: g.dma_start(tmpf[rsl][:], x1f_h[ti * 128:(ti + 1) * 128, :]), inc=s_x1r[rsl], k=16)
            tmpf_slot[rsl].w = e_xr
            ysl = tl
            sc.waits("vector", y2_evs[tl])
            sc.wait("vector", e_xr)
            e_y = sc.comp("vector", lambda v, ysl=ysl, rsl=rsl: v.tensor_add(y2[ysl][:], y2[ysl][:], tmpf[rsl][:]), inc=s_ve)
            tmpf_slot[rsl].r = [e_y]
            e_n = ln_chain(e_y, y2[ysl], y2[ysl], ti)
            y2_slot[ysl].w = e_n
            sc.wait("scalar", e_n)
            e_o = sc.run("scalar", lambda s, ti=ti, ysl=ysl: s.dma_start(out_h[ti * 128:(ti + 1) * 128, :], y2[ysl][:]), inc=s_out[ysl], k=16)
            y2_slot[ysl].r = [e_o]

    for so in s_out:
        sc.wait("scalar", (so, so.n))
    if dbg >= 1:
        sc.wait("scalar", (s_x1st[0], s_x1st[0].n))
        sc.wait("scalar", (s_x1st[1], s_x1st[1].n))
    if dbg >= 2:
        sc.wait("scalar", (s_dbg, s_dbg.n))

    with nc.Block() as block:
        @block.tensor
        def _(eng):
            for f in sc.ops["tensor"]:
                f(eng)

        @block.vector
        def _(eng):
            for f in sc.ops["vector"]:
                f(eng)

        @block.scalar
        def _(eng):
            for f in sc.ops["scalar"]:
                f(eng)

        @block.sync
        def _(eng):
            for f in sc.ops["sync"]:
                f(eng)

        @block.gpsimd
        def _(eng):
            for f in sc.ops["gpsimd"]:
                f(eng)

    es.close()
    return nc


def _host_inputs(L0_attn_x, L0_attn_filt, L0_ffn_w1_w, L0_ffn_w1_s, L0_ffn_w2_w,
                 L0_ffn_w2_s, L0_ffn_h1_w, L0_ffn_h1_s, L0_ffn_h2_w, L0_ffn_h2_s,
                 L0_ffn_w3_w, L0_ffn_w3_s):
    x = np.ascontiguousarray(np.asarray(L0_attn_x, dtype=np.float32))
    filt = np.asarray(L0_attn_filt, dtype=np.float32)
    cc, ss, tt = np.meshgrid(np.arange(WIN), np.arange(128), np.arange(128), indexing="ij")
    jj = 128 * cc + ss - tt
    F = np.where((jj >= 0) & (jj < L), filt[np.clip(jj, 0, L - 1)], np.float32(0)).astype(np.float32)
    resh = lambda s: np.ascontiguousarray(np.asarray(s, np.float32).reshape(-1, 128).T)
    common = {
        "Ftp": F,
        "w1": np.ascontiguousarray(np.asarray(L0_ffn_w1_w, np.float32).T),
        "w2": np.ascontiguousarray(np.asarray(L0_ffn_w2_w, np.float32).T),
        "h1": np.ascontiguousarray(np.asarray(L0_ffn_h1_w, np.float32).T),
        "h2": np.ascontiguousarray(np.asarray(L0_ffn_h2_w, np.float32).T),
        "w3": np.ascontiguousarray(np.asarray(L0_ffn_w3_w, np.float32).T),
        "s1r": resh(L0_ffn_w1_s),
        "s2r": resh(L0_ffn_w2_s),
        "sh1r": resh(L0_ffn_h1_s),
        "sh2r": resh(L0_ffn_h2_s),
        "s3b": np.ascontiguousarray(np.broadcast_to(np.asarray(L0_ffn_w3_s, np.float32)[None, :], (128, D))),
    }
    in_maps = []
    for c in range(8):
        b, h = c // 2, c % 2
        xc = np.zeros((33 * 128, D), np.float32)
        src = x[b, h * TOK:]
        xc[:src.shape[0] if src.shape[0] <= 33 * 128 else 33 * 128] = src[:33 * 128]
        m = dict(common)
        m["x"] = xc
        in_maps.append(m)
    return in_maps


def kernel(**inputs):
    in_maps = _host_inputs(**inputs)
    nc = build_nc(dbg=0)
    res = run_bass_kernel_spmd(nc, in_maps, core_ids=list(range(8)))
    out = np.empty((4, 4096, D), np.float32)
    for c in range(8):
        b, h = c // 2, c % 2
        out[b, h * TOK:(h + 1) * TOK] = res.results[c]["out"]
    return out


# version 22
# speedup vs baseline: 1.3554x; 1.3554x over previous
import numpy as np
import contextlib

import concourse.bass as bass
import concourse.mybir as mybir
from concourse.bass_utils import run_bass_kernel_spmd

F32 = mybir.dt.float32
BF16 = mybir.dt.bfloat16
AF = mybir.ActivationFunctionType
ALU = mybir.AluOpType
AX = mybir.AxisListType

TOK = 2048
D = 1024
HID = 4096
L = 2048
NT = TOK // 128
NCH = 32
WIN = 17
RING = 18
NQ = 4
TQ = TOK // NQ
NTQ = TQ // 128

ENGINES = ("tensor", "vector", "scalar", "sync", "gpsimd")


class Sem:
    __slots__ = ("h", "n", "name")

    def __init__(self, es, nc, name):
        self.h = es.enter_context(nc.semaphore(name))
        self.n = 0
        self.name = name


class Sched:
    def __init__(self):
        self.ops = {e: [] for e in ENGINES}
        self.lastwait = {e: {} for e in ENGINES}

    def wait(self, e, ev):
        if ev is None:
            return
        sem, v = ev
        if v <= 0:
            return
        lw = self.lastwait[e]
        if lw.get(sem, 0) >= v:
            return
        lw[sem] = v
        self.ops[e].append(lambda eng, s=sem, vv=v: eng.wait_ge(s.h, vv))

    def waits(self, e, evs):
        for ev in evs:
            self.wait(e, ev)

    def comp(self, e, fn, inc=None, k=1):
        self.ops[e].append(lambda eng: eng.drain())
        return self.run(e, fn, inc, k)

    def run(self, e, fn, inc=None, k=1):
        if inc is not None:
            inc.n += k
            v = inc.n
            self.ops[e].append(lambda eng, f=fn, s=inc, kk=k: f(eng).then_inc(s.h, kk))
            return (inc, v)
        self.ops[e].append(fn)
        return None


class Slot:
    __slots__ = ("w", "r")

    def __init__(self):
        self.w = None
        self.r = []


def build_nc(dbg=0):
    nc = bass.Bass("TRN2", target_bir_lowering=False)

    x_h = nc.dram_tensor("x", [33 * 128, D], F32, kind="ExternalInput").ap()
    F_h = nc.dram_tensor("Ftp", [WIN, 128, 128], F32, kind="ExternalInput").ap()
    w1_h = nc.dram_tensor("w1", [D, HID], F32, kind="ExternalInput").ap()
    w2_h = nc.dram_tensor("w2", [D, HID], F32, kind="ExternalInput").ap()
    h1_h = nc.dram_tensor("h1", [HID, HID], F32, kind="ExternalInput").ap()
    h2_h = nc.dram_tensor("h2", [HID, HID], F32, kind="ExternalInput").ap()
    w3_h = nc.dram_tensor("w3", [HID, D], F32, kind="ExternalInput").ap()
    s1_h = nc.dram_tensor("s1r", [128, HID // 128], F32, kind="ExternalInput").ap()
    s2_h = nc.dram_tensor("s2r", [128, HID // 128], F32, kind="ExternalInput").ap()
    sh1_h = nc.dram_tensor("sh1r", [128, HID // 128], F32, kind="ExternalInput").ap()
    sh2_h = nc.dram_tensor("sh2r", [128, HID // 128], F32, kind="ExternalInput").ap()
    s3b_h = nc.dram_tensor("s3b", [128, D], F32, kind="ExternalInput").ap()
    out_h = nc.dram_tensor("out", [TOK, D], F32, kind="ExternalOutput").ap()

    w1s = nc.dram_tensor("w1s", [HID // 512, 128, D // 128, 512], BF16, kind="Internal").ap()
    w2s = nc.dram_tensor("w2s", [HID // 512, 128, D // 128, 512], BF16, kind="Internal").ap()
    h1s = nc.dram_tensor("h1s", [HID // 256, 128, HID // 128, 256], BF16, kind="Internal").ap()
    h2s = nc.dram_tensor("h2s", [HID // 256, 128, HID // 128, 256], BF16, kind="Internal").ap()
    w3s = nc.dram_tensor("w3s", [D // 256, 128, HID // 128, 256], BF16, kind="Internal").ap()
    x1bf = [nc.dram_tensor(f"x1bf{i}", [TOK, 128], BF16, kind="Internal").ap() for i in range(D // 128)]
    if dbg >= 1:
        x1f_h = nc.dram_tensor("x1f", [TOK, D], F32, kind="ExternalOutput").ap()
    else:
        x1f_h = nc.dram_tensor("x1f", [TOK, D], F32, kind="Internal").ap()
    dbg_g = nc.dram_tensor("dbg_g", [128, HID // 128 * TQ], BF16, kind="ExternalOutput").ap() if dbg >= 2 else None
    dbg_u = nc.dram_tensor("dbg_u", [128, HID // 128 * TQ], BF16, kind="ExternalOutput").ap() if dbg >= 3 else None
    dbg_v = nc.dram_tensor("dbg_v", [128, HID // 128 * TQ], BF16, kind="ExternalOutput").ap() if dbg >= 4 else None

    es = contextlib.ExitStack()
    sc = Sched()

    S = lambda name: Sem(es, nc, name)
    s_ld = S("s_ld")
    s_xst = [S(f"s_xst{i}") for i in range(2)]
    s_xres = [S(f"s_xres{i}") for i in range(2)]
    s_x1r = [S(f"s_x1r{i}") for i in range(2)]
    s_wld = [S(f"s_wld{i}") for i in range(5)]
    s_wwr = [S(f"s_wwr{i}") for i in range(5)]
    s_tx1 = S("s_tx1")
    s_tw1 = [S(f"s_tw1_{i}") for i in range(2)]
    s_tw2 = [S(f"s_tw2_{i}") for i in range(2)]
    s_th = [S(f"s_th{i}") for i in range(2)]
    s_tw3 = S("s_tw3")
    s_x1st = [S(f"s_x1st{i}") for i in range(2)]
    s_x1bst = [S(f"s_x1bst{i}") for i in range(2)]
    s_out = [S(f"s_out{i}") for i in range(4)]
    s_dbg = S("s_dbg")
    s_pe = S("s_pe")
    s_ve = S("s_ve")
    s_sc = S("s_sc")

    sb = lambda name, shape, dt: es.enter_context(nc.sbuf_tensor(name, shape, dt))
    cpool = sb("cpool", [128, 4], F32)
    sc_s1 = sb("sc_s1", [128, 32], F32)
    sc_s2 = sb("sc_s2", [128, 32], F32)
    sc_h1 = sb("sc_h1", [128, 32], F32)
    sc_h2 = sb("sc_h2", [128, 32], F32)
    s3bt = sb("s3bt", [128, D], F32)
    stats = sb("stats", [128, 16], F32)
    sqs = sb("sqs", [128, D // 2], F32)
    x1T = sb("x1T", [128, (D // 128) * TQ], BF16)
    wsl1 = [sb(f"wsl1_{i}", [128, (D // 128) * 512], BF16) for i in range(2)]
    wsl2 = [sb(f"wsl2_{i}", [128, (D // 128) * 512], BF16) for i in range(2)]
    hsl = [sb(f"hsl{i}", [128, (HID // 128) * 256], BF16) for i in range(2)]
    tmpf = [sb(f"tmpf{i}", [128, D], F32) for i in range(2)]
    wstgf = [sb(f"wstgf{i}", [128, 512], F32) for i in range(5)]
    wstgb = [sb(f"wstgb{i}", [128, 512], BF16) for i in range(5)]

    ps = [es.enter_context(nc.psum_tensor(f"ps{i}", [128, 512], F32)) for i in range(8)]
    bank_evac = [None] * 8
    bank_group = [None] * 8

    def pe_group(bank, mms, width=512):
        sc.wait("tensor", bank_evac[bank])
        n = len(mms)
        out_ap = ps[bank][:] if width == 512 else ps[bank][:, 0:width]
        for i, (lt, rh) in enumerate(mms):
            st = (i == 0)
            sp = (i == n - 1)
            if sp:
                ev = sc.run("tensor", lambda t, a=lt, b=rh, o=out_ap, s=st: t.matmul(o, a, b, start=s, stop=True), inc=s_pe)
            else:
                sc.run("tensor", lambda t, a=lt, b=rh, o=out_ap, s=st: t.matmul(o, a, b, start=s, stop=False))
        bank_group[bank] = ev
        return ev

    es_conv = contextlib.ExitStack()
    cb = lambda name, shape, dt: es_conv.enter_context(nc.sbuf_tensor(name, shape, dt))
    xwh = [cb(f"xwh{i}", [128, D], BF16) for i in range(RING)]
    xwl = [cb(f"xwl{i}", [128, D], BF16) for i in range(RING)]
    xstg = [cb(f"xstg{i}", [128, D], F32) for i in range(2)]
    xres = [cb(f"xres{i}", [128, D], F32) for i in range(2)]
    x1ft = [cb(f"x1ft{i}", [128, D], F32) for i in range(2)]
    Fh = cb("Fh", [128, WIN * 128], BF16)
    Fl = cb("Fl", [128, WIN * 128], BF16)

    sc.run("gpsimd", lambda g: g.dma_start(sc_s1[:], s1_h), inc=s_ld, k=16)
    sc.run("gpsimd", lambda g: g.dma_start(sc_s2[:], s2_h), inc=s_ld, k=16)
    sc.run("gpsimd", lambda g: g.dma_start(sc_h1[:], sh1_h), inc=s_ld, k=16)
    sc.run("gpsimd", lambda g: g.dma_start(sc_h2[:], sh2_h), inc=s_ld, k=16)
    sc.run("gpsimd", lambda g: g.dma_start(s3bt[:], s3b_h), inc=s_ld, k=16)
    ld_total = (s_ld, s_ld.n)

    wstgf_slot = [Slot() for _ in range(5)]
    wstgb_slot = [Slot() for _ in range(5)]

    sc.comp("vector", lambda v: v.memset(cpool[:, 0:1], -0.5))
    sc.comp("vector", lambda v: v.memset(cpool[:, 1:2], 1e-5))
    sc.comp("vector", lambda v: v.memset(cpool[:, 2:3], 1.0 / D))
    sc.wait("vector", ld_total)
    fprep = None
    for c in range(WIN):
        sl = c % 5
        sc.waits("gpsimd", wstgf_slot[sl].r)
        e_l = sc.run("gpsimd", lambda g, c=c, sl=sl: g.dma_start(wstgf[sl][:, 0:128], F_h[c, :, :]), inc=s_wld[sl], k=16)
        wstgf_slot[sl].w = e_l
        wstgf_slot[sl].r = []
        sc.wait("vector", e_l)
        sc.comp("vector", lambda v, c=c, sl=sl: v.tensor_copy(Fh[:, c * 128:(c + 1) * 128], wstgf[sl][:, 0:128]))
        fprep = sc.comp("vector", lambda v, c=c, sl=sl: v.tensor_sub(Fl[:, c * 128:(c + 1) * 128], wstgf[sl][:, 0:128], Fh[:, c * 128:(c + 1) * 128]), inc=s_ve)
        wstgf_slot[sl].r = [fprep]

    cast_jobs = []
    cast_range = {}
    def cast_weight(name, srch, dst, I, O, obw, wq=None):
        i0 = len(cast_jobs)
        nic = I // 128
        for c0 in range(0, O, 512):
            w = min(512, O - c0)
            for ic in range(nic):
                cast_jobs.append((name, srch, dst, ic, c0, w, obw, ic == nic - 1, wq))
        cast_range[name] = (i0, len(cast_jobs))
    cast_weight("w1", w1_h, w1s, D, HID, 512)
    cast_weight("w2", w2_h, w2s, D, HID, 512)
    cast_weight("h1", h1_h, h1s, HID, HID, 256)
    cast_weight("h2", h2_h, h2s, HID, HID, 256, wq="scalar")
    cast_weight("w3", w3_h, w3s, HID, D, 256, wq="scalar")
    cast_state = {"i": 0, "wq": 0}
    cast_grp = {}
    def emit_casts(k):
        for _ in range(k):
            i = cast_state["i"]
            if i >= len(cast_jobs):
                return
            cast_state["i"] += 1
            name, srch, dst, ic, c0, w, obw, grp_end, wq = cast_jobs[i]
            sl = i % 5
            sc.waits("gpsimd", wstgf_slot[sl].r)
            e_l = sc.run("gpsimd", lambda g, srch=srch, ic=ic, c0=c0, w=w, sl=sl:
                         g.dma_start(wstgf[sl][:, 0:w], srch[ic * 128:(ic + 1) * 128, c0:c0 + w]), inc=s_wld[sl], k=16)
            wstgf_slot[sl].w = e_l
            wstgf_slot[sl].r = []
            sc.wait("vector", e_l)
            sc.waits("vector", wstgb_slot[sl].r)
            e_c = sc.comp("vector", lambda v, w=w, sl=sl: v.tensor_copy(wstgb[sl][:, 0:w], wstgf[sl][:, 0:w]), inc=s_ve)
            wstgf_slot[sl].r = [e_c]
            wstgb_slot[sl].w = e_c
            wstgb_slot[sl].r = []
            last = None
            for j in range(w // obw):
                ob = c0 // obw + j
                if wq is None:
                    qe = "sync" if (cast_state["wq"] % 2 == 0) else "scalar"
                    cast_state["wq"] += 1
                else:
                    qe = wq
                sc.wait(qe, e_c)
                last = sc.run(qe, lambda s, dst=dst, ob=ob, ic=ic, sl=sl, j=j, obw=obw:
                              s.dma_start(dst[ob, :, ic, :], wstgb[sl][:, j * obw:(j + 1) * obw]), inc=s_wwr[sl], k=16)
            wstgb_slot[sl].r = [last]
            if grp_end:
                cast_grp[(name, c0)] = [(sw, sw.n) for sw in s_wwr]
    def emit_casts_until(name):
        emit_casts(cast_range[name][1] - cast_state["i"])
    def cast_grp_ev(name, ob, obw):
        return cast_grp[(name, (ob * obw // 512) * 512)]

    xst_slot = [Slot() for _ in range(2)]
    ring_h = [Slot() for _ in range(RING)]
    ring_l = [Slot() for _ in range(RING)]
    xres_slot = [Slot() for _ in range(2)]
    x1ft_slot = [Slot() for _ in range(2)]
    stats_slot = [Slot() for _ in range(2)]

    def load_chunk(w):
        sl = w % 2
        rs = w % RING
        sc.waits("scalar", xst_slot[sl].r)
        ev = sc.run("scalar", lambda g, w=w, sl=sl: g.dma_start(xstg[sl][:], x_h[w * 128:(w + 1) * 128, :]), inc=s_xst[sl], k=16)
        xst_slot[sl].w = ev
        xst_slot[sl].r = []
        sc.wait("vector", ev)
        sc.waits("vector", ring_h[rs].r)
        sc.waits("vector", ring_l[rs].r)
        e1 = sc.comp("vector", lambda v, rs=rs, sl=sl: v.tensor_copy(xwh[rs][:], xstg[sl][:]), inc=s_ve)
        e2 = sc.comp("vector", lambda v, rs=rs, sl=sl: v.tensor_sub(xwl[rs][:], xstg[sl][:], xwh[rs][:]), inc=s_ve)
        xst_slot[sl].r.extend([e1, e2])
        ring_h[rs].w = e1
        ring_h[rs].r = []
        ring_l[rs].w = e2
        ring_l[rs].r = []

    def ln_chain(engine_y_ev, ytile, dst_f32, tile_idx):
        st = tile_idx % 2
        so = st * 8
        sc.waits("vector", stats_slot[st].r)
        stats_slot[st].r = []
        sc.wait("vector", engine_y_ev)
        sc.comp("vector", lambda v: v.tensor_reduce(stats[:, so:so + 1], ytile[:], AX.X, ALU.add))
        sc.comp("vector", lambda v: v.tensor_tensor(sqs[:], ytile[:, 0:D // 2], ytile[:, 0:D // 2], ALU.mult))
        sc.comp("vector", lambda v: v.tensor_reduce(stats[:, so + 1:so + 2], sqs[:], AX.X, ALU.add))
        sc.comp("vector", lambda v: v.tensor_tensor(sqs[:], ytile[:, D // 2:D], ytile[:, D // 2:D], ALU.mult))
        sc.comp("vector", lambda v: v.tensor_reduce(stats[:, so + 2:so + 3], sqs[:], AX.X, ALU.add))
        sc.comp("vector", lambda v: v.tensor_add(stats[:, so + 1:so + 2], stats[:, so + 1:so + 2], stats[:, so + 2:so + 3]))
        sc.comp("vector", lambda v: v.tensor_scalar(stats[:, so + 2:so + 3], stats[:, so:so + 1], cpool[:, 2:3], None, ALU.mult))
        sc.comp("vector", lambda v: v.tensor_scalar(stats[:, so + 3:so + 4], stats[:, so + 1:so + 2], cpool[:, 2:3], cpool[:, 1:2], ALU.mult, ALU.add))
        sc.comp("vector", lambda v: v.tensor_tensor(stats[:, so + 4:so + 5], stats[:, so + 2:so + 3], stats[:, so + 2:so + 3], ALU.mult))
        e_v2 = sc.comp("vector", lambda v: v.tensor_sub(stats[:, so + 5:so + 6], stats[:, so + 3:so + 4], stats[:, so + 4:so + 5]), inc=s_ve)
        sc.wait("scalar", e_v2)
        sc.comp("scalar", lambda s: s.activation(stats[:, so + 6:so + 7], stats[:, so + 5:so + 6], AF.Ln))
        e_rs = sc.comp("scalar", lambda s: s.activation(stats[:, so + 7:so + 8], stats[:, so + 6:so + 7], AF.Exp, scale=cpool[:, 0:1]), inc=s_sc)
        sc.wait("vector", e_rs)
        e_n = sc.comp("vector", lambda v: v.tensor_scalar(dst_f32[:], ytile[:], stats[:, so + 2:so + 3], stats[:, so + 7:so + 8], ALU.subtract, ALU.mult), inc=s_ve)
        stats_slot[st].r = [e_n]
        return e_n

    for w in range(WIN):
        load_chunk(w)
        emit_casts(1)

    for ti in range(NT):
        if ti + WIN < NCH:
            load_chunk(ti + WIN)
        rsl = ti % 2
        sc.waits("gpsimd", xres_slot[rsl].r)
        e_res = sc.run("gpsimd", lambda g, ti=ti, rsl=rsl: g.dma_start(xres[rsl][:], x_h[ti * 128:(ti + 1) * 128, :]), inc=s_xres[rsl], k=16)
        xres_slot[rsl].w = e_res
        xres_slot[rsl].r = []
        emit_casts(32)

        sc.wait("tensor", fprep)
        for cc in range(WIN):
            sc.wait("tensor", ring_h[(ti + cc) % RING].w)
            sc.wait("tensor", ring_l[(ti + cc) % RING].w)
        grp_ev = [None, None]
        for g in range(2):
            bank = (ti % 2) * 2 + g
            mms = []
            for cc in range(WIN):
                rs = (ti + cc) % RING
                mms.append((Fh[:, cc * 128:(cc + 1) * 128], xwh[rs][:, g * 512:(g + 1) * 512]))
                mms.append((Fh[:, cc * 128:(cc + 1) * 128], xwl[rs][:, g * 512:(g + 1) * 512]))
            for cc in range(WIN):
                rs = (ti + cc) % RING
                mms.append((Fl[:, cc * 128:(cc + 1) * 128], xwh[rs][:, g * 512:(g + 1) * 512]))
            grp_ev[g] = pe_group(bank, mms)
        for cc in range(WIN):
            rs = (ti + cc) % RING
            ring_h[rs].r = [grp_ev[1]]
            ring_l[rs].r = [grp_ev[1]]

        fsl = ti % 2
        sc.waits("vector", x1ft_slot[fsl].r)
        x1ft_slot[fsl].r = []
        sc.wait("vector", e_res)
        ev_y = None
        for g in range(2):
            bank = (ti % 2) * 2 + g
            sc.wait("vector", grp_ev[g])
            ev_y = sc.comp("vector", lambda v, b=bank, g=g, fsl=fsl, rr=rsl: v.tensor_add(x1ft[fsl][:, g * 512:(g + 1) * 512], ps[b][:], xres[rr][:, g * 512:(g + 1) * 512]), inc=s_ve)
            bank_evac[bank] = ev_y
        xres_slot[rsl].r = [ev_y]
        e_n = ln_chain(ev_y, x1ft[fsl], x1ft[fsl], ti)
        x1ft_slot[fsl].w = e_n

        sc.wait("scalar", e_n)
        e_st = sc.run("scalar", lambda s, ti=ti, fsl=fsl: s.dma_start(x1f_h[ti * 128:(ti + 1) * 128, :], x1ft[fsl][:]), inc=s_x1st[fsl], k=16)
        x1ft_slot[fsl].r.append(e_st)
        sc.wait("gpsimd", e_n)
        last_b = None
        for ic in range(D // 128):
            last_b = sc.run("gpsimd", lambda g, ti=ti, ic=ic, fsl=fsl: g.dma_start(x1bf[ic][ti * 128:(ti + 1) * 128, :], x1ft[fsl][:, ic * 128:(ic + 1) * 128]), inc=s_x1bst[fsl], k=16)
        x1ft_slot[fsl].r.append(last_b)

    conv_pe_done = bank_group[:]

    x1T_slot = Slot()
    wsl1_slot = [Slot() for _ in range(2)]
    wsl2_slot = [Slot() for _ in range(2)]
    hsl_slot = [Slot() for _ in range(2)]
    tmpf_slot = [Slot() for _ in range(2)]
    y2_slot = [Slot() for _ in range(4)]

    NOG = HID // 128

    es_conv.close()
    gT = sb("gT", [128, NOG * TQ], BF16)
    uT = sb("uT", [128, NOG * TQ], BF16)
    y2 = [sb(f"y2_{i}", [128, D], F32) for i in range(4)]
    gT_w = [None] * NOG
    uT_w = [None] * NOG
    gT_rd = None
    uT_rd = None

    def dump_sbuf(tag, src_ap, dst_h, ready_evs):
        sc.waits("scalar", ready_evs)
        sc.run("scalar", lambda s: s.dma_start(dst_h, src_ap[:]), inc=s_dbg, k=16)
        sc.wait("scalar", (s_dbg, s_dbg.n))
        sc.wait("vector", (s_dbg, s_dbg.n))
        sc.wait("tensor", (s_dbg, s_dbg.n))

    emit_casts_until("w2")

    for q in range(NQ):
        sc.waits("sync", x1T_slot.r)
        x1T_slot.r = []
        sc.wait("sync", (s_x1bst[0], s_x1bst[0].n))
        sc.wait("sync", (s_x1bst[1], s_x1bst[1].n))
        for ic in range(D // 128):
            e = sc.run("sync", lambda s, ic=ic, q=q: s.dma_start(x1T[:, ic * TQ:(ic + 1) * TQ], x1bf[ic][q * TQ:(q + 1) * TQ, :], transpose=True), inc=s_tx1, k=16)
        x1T_slot.w = (s_tx1, s_tx1.n)

        for ob in range(HID // 512):
            sl = ob % 2
            sc.waits("sync", cast_grp_ev("w1", ob, 512))
            sc.waits("sync", cast_grp_ev("w2", ob, 512))
            sc.waits("sync", wsl1_slot[sl].r)
            wsl1_slot[sl].r = []
            sc.run("sync", lambda s, ob=ob, sl=sl: s.dma_start(wsl1[sl][:], w1s[ob, :, :, :]), inc=s_tw1[sl], k=16)
            wsl1_slot[sl].w = (s_tw1[sl], s_tw1[sl].n)
            sc.waits("sync", wsl2_slot[sl].r)
            wsl2_slot[sl].r = []
            sc.run("sync", lambda s, ob=ob, sl=sl: s.dma_start(wsl2[sl][:], w2s[ob, :, :, :]), inc=s_tw2[sl], k=16)
            wsl2_slot[sl].w = (s_tw2[sl], s_tw2[sl].n)

            sc.wait("tensor", x1T_slot.w)
            sc.wait("tensor", wsl1_slot[sl].w)
            sc.wait("tensor", wsl2_slot[sl].w)
            for ogl in range(4):
                og = ob * 4 + ogl
                b1 = (og % 2) * 4
                b2 = b1 + 1
                mm1 = [(wsl1[sl][:, ic * 512 + ogl * 128: ic * 512 + ogl * 128 + 128], x1T[:, ic * TQ:(ic + 1) * TQ]) for ic in range(D // 128)]
                mm2 = [(wsl2[sl][:, ic * 512 + ogl * 128: ic * 512 + ogl * 128 + 128], x1T[:, ic * TQ:(ic + 1) * TQ]) for ic in range(D // 128)]
                e1 = pe_group(b1, mm1)
                e2 = pe_group(b2, mm2)
                wsl1_slot[sl].r = [e1]
                wsl2_slot[sl].r = [e2]
                x1T_slot.r = [e2]
                tsl = og % 2
                sc.waits("scalar", tmpf_slot[tsl].r)
                tmpf_slot[tsl].r = []
                sc.wait("scalar", e1)
                e_sil = sc.comp("scalar", lambda s, b=b1, og=og, tsl=tsl: s.activation(tmpf[tsl][:, 0:512], ps[b][:], AF.Silu, scale=sc_s1[:, og:og + 1]), inc=s_sc)
                bank_evac[b1] = e_sil
                tmpf_slot[tsl].w = e_sil
                sc.wait("vector", e2)
                sc.wait("vector", e_sil)
                e_g = sc.comp("vector", lambda v, b=b2, og=og, tsl=tsl: v.scalar_tensor_tensor(gT[:, og * TQ:(og + 1) * TQ], ps[b][:], sc_s2[:, og:og + 1], tmpf[tsl][:, 0:512], ALU.mult, ALU.mult), inc=s_ve)
                bank_evac[b2] = e_g
                tmpf_slot[tsl].r = [e_g]
                gT_w[og] = e_g
        if dbg >= 2 and q == 0:
            dump_sbuf("g", gT, dbg_g, list(gT_w))

        if q == 0:
            emit_casts_until("h1")
        uT_w_new = [None] * NOG
        for ob in range(HID // 256):
            sl = ob % 2
            sc.waits("sync", cast_grp_ev("h1", ob, 256))
            sc.waits("sync", hsl_slot[sl].r)
            hsl_slot[sl].r = []
            sc.run("sync", lambda s, ob=ob, sl=sl: s.dma_start(hsl[sl][:], h1s[ob, :, :, :]), inc=s_th[sl], k=16)
            hsl_slot[sl].w = (s_th[sl], s_th[sl].n)
            sc.wait("tensor", hsl_slot[sl].w)
            last_e = None
            for ogl in range(2):
                og = ob * 2 + ogl
                bank = (2, 3, 6, 7)[og % 4]
                sc.waits("tensor", [gT_w[ic] for ic in range(NOG)])
                mms = [(hsl[sl][:, ic * 256 + ogl * 128: ic * 256 + ogl * 128 + 128], gT[:, ic * TQ:(ic + 1) * TQ]) for ic in range(NOG)]
                e = pe_group(bank, mms)
                last_e = e
                sc.wait("scalar", e)
                e_u = sc.comp("scalar", lambda s, b=bank, og=og: s.activation(uT[:, og * TQ:(og + 1) * TQ], ps[b][:], AF.Gelu, scale=sc_h1[:, og:og + 1]), inc=s_sc)
                bank_evac[bank] = e_u
                uT_w_new[og] = e_u
            hsl_slot[sl].r = [last_e]
            if q == 0:
                emit_casts(16)
        uT_w = uT_w_new
        gT_rd = (s_pe, s_pe.n)
        if dbg >= 3 and q == 0:
            dump_sbuf("u", uT, dbg_u, list(uT_w))

        for ob in range(HID // 256):
            sl = ob % 2
            sc.waits("sync", cast_grp_ev("h2", ob, 256))
            sc.waits("sync", hsl_slot[sl].r)
            hsl_slot[sl].r = []
            sc.run("sync", lambda s, ob=ob, sl=sl: s.dma_start(hsl[sl][:], h2s[ob, :, :, :]), inc=s_th[sl], k=16)
            hsl_slot[sl].w = (s_th[sl], s_th[sl].n)
            sc.wait("tensor", hsl_slot[sl].w)
            last_e = None
            for ogl in range(2):
                og = ob * 2 + ogl
                bank = (2, 3, 6, 7)[og % 4]
                sc.waits("tensor", [uT_w[ic] for ic in range(NOG)])
                sc.wait("tensor", gT_rd)
                mms = [(hsl[sl][:, ic * 256 + ogl * 128: ic * 256 + ogl * 128 + 128], uT[:, ic * TQ:(ic + 1) * TQ]) for ic in range(NOG)]
                e = pe_group(bank, mms)
                last_e = e
                sc.wait("scalar", e)
                e_v = sc.comp("scalar", lambda s, b=bank, og=og: s.activation(gT[:, og * TQ:(og + 1) * TQ], ps[b][:], AF.Gelu, scale=sc_h2[:, og:og + 1]), inc=s_sc)
                bank_evac[bank] = e_v
                gT_w[og] = e_v
            hsl_slot[sl].r = [last_e]
            if q == 0:
                emit_casts(4)
        uT_rd = (s_pe, s_pe.n)
        if dbg >= 4 and q == 0:
            dump_sbuf("v", gT, dbg_v, list(gT_w))

        y2_evs = [[None] * (D // 256) for _ in range(NTQ)]
        for ob in range(D // 256):
            sl = ob % 2
            sc.waits("sync", cast_grp_ev("w3", ob, 256))
            sc.waits("sync", hsl_slot[sl].r)
            hsl_slot[sl].r = []
            sc.run("sync", lambda s, ob=ob, sl=sl: s.dma_start(hsl[sl][:], w3s[ob, :, :, :]), inc=s_th[sl], k=16)
            hsl_slot[sl].w = (s_th[sl], s_th[sl].n)
            sc.wait("tensor", hsl_slot[sl].w)
            last_e = None
            for tl in range(NTQ):
                bank = (tl % 4) * 2 + (ob % 2)
                sc.waits("tensor", [gT_w[icx] for icx in range(NOG)])
                mms = [(gT[:, ic * TQ + tl * 128: ic * TQ + tl * 128 + 128], hsl[sl][:, ic * 256:(ic + 1) * 256]) for ic in range(NOG)]
                e = pe_group(bank, mms, width=256)
                last_e = e
                ysl = tl
                if ob == 0:
                    sc.waits("vector", y2_slot[ysl].r)
                    y2_slot[ysl].r = []
                sc.wait("vector", e)
                e_c = sc.comp("vector", lambda v, b=bank, ob=ob, ysl=ysl: v.tensor_mul(y2[ysl][:, ob * 256:(ob + 1) * 256], ps[b][:, 0:256], s3bt[:, ob * 256:(ob + 1) * 256]), inc=s_ve)
                bank_evac[bank] = e_c
                y2_evs[tl][ob] = e_c
            hsl_slot[sl].r = [last_e]
        for tl in range(NTQ):
            ti = q * NTQ + tl
            rsl = tl % 2
            sc.waits("gpsimd", tmpf_slot[rsl].r)
            tmpf_slot[rsl].r = []
            sc.wait("gpsimd", (s_x1st[0], s_x1st[0].n))
            sc.wait("gpsimd", (s_x1st[1], s_x1st[1].n))
            e_xr = sc.run("gpsimd", lambda g, ti=ti, rsl=rsl: g.dma_start(tmpf[rsl][:], x1f_h[ti * 128:(ti + 1) * 128, :]), inc=s_x1r[rsl], k=16)
            tmpf_slot[rsl].w = e_xr
            ysl = tl
            sc.waits("vector", y2_evs[tl])
            sc.wait("vector", e_xr)
            e_y = sc.comp("vector", lambda v, ysl=ysl, rsl=rsl: v.tensor_add(y2[ysl][:], y2[ysl][:], tmpf[rsl][:]), inc=s_ve)
            tmpf_slot[rsl].r = [e_y]
            e_n = ln_chain(e_y, y2[ysl], y2[ysl], ti)
            y2_slot[ysl].w = e_n
            sc.wait("scalar", e_n)
            e_o = sc.run("scalar", lambda s, ti=ti, ysl=ysl: s.dma_start(out_h[ti * 128:(ti + 1) * 128, :], y2[ysl][:]), inc=s_out[ysl], k=16)
            y2_slot[ysl].r = [e_o]

    for so in s_out:
        sc.wait("scalar", (so, so.n))
    if dbg >= 1:
        sc.wait("scalar", (s_x1st[0], s_x1st[0].n))
        sc.wait("scalar", (s_x1st[1], s_x1st[1].n))
    if dbg >= 2:
        sc.wait("scalar", (s_dbg, s_dbg.n))

    with nc.Block() as block:
        @block.tensor
        def _(eng):
            for f in sc.ops["tensor"]:
                f(eng)

        @block.vector
        def _(eng):
            for f in sc.ops["vector"]:
                f(eng)

        @block.scalar
        def _(eng):
            for f in sc.ops["scalar"]:
                f(eng)

        @block.sync
        def _(eng):
            for f in sc.ops["sync"]:
                f(eng)

        @block.gpsimd
        def _(eng):
            for f in sc.ops["gpsimd"]:
                f(eng)

    es.close()
    return nc


def _host_inputs(L0_attn_x, L0_attn_filt, L0_ffn_w1_w, L0_ffn_w1_s, L0_ffn_w2_w,
                 L0_ffn_w2_s, L0_ffn_h1_w, L0_ffn_h1_s, L0_ffn_h2_w, L0_ffn_h2_s,
                 L0_ffn_w3_w, L0_ffn_w3_s):
    x = np.ascontiguousarray(np.asarray(L0_attn_x, dtype=np.float32))
    filt = np.asarray(L0_attn_filt, dtype=np.float32)
    cc, ss, tt = np.meshgrid(np.arange(WIN), np.arange(128), np.arange(128), indexing="ij")
    jj = 128 * cc + ss - tt
    F = np.where((jj >= 0) & (jj < L), filt[np.clip(jj, 0, L - 1)], np.float32(0)).astype(np.float32)
    resh = lambda s: np.ascontiguousarray(np.asarray(s, np.float32).reshape(-1, 128).T)
    common = {
        "Ftp": F,
        "w1": np.ascontiguousarray(np.asarray(L0_ffn_w1_w, np.float32).T),
        "w2": np.ascontiguousarray(np.asarray(L0_ffn_w2_w, np.float32).T),
        "h1": np.ascontiguousarray(np.asarray(L0_ffn_h1_w, np.float32).T),
        "h2": np.ascontiguousarray(np.asarray(L0_ffn_h2_w, np.float32).T),
        "w3": np.ascontiguousarray(np.asarray(L0_ffn_w3_w, np.float32).T),
        "s1r": resh(L0_ffn_w1_s),
        "s2r": resh(L0_ffn_w2_s),
        "sh1r": resh(L0_ffn_h1_s),
        "sh2r": resh(L0_ffn_h2_s),
        "s3b": np.ascontiguousarray(np.broadcast_to(np.asarray(L0_ffn_w3_s, np.float32)[None, :], (128, D))),
    }
    in_maps = []
    for c in range(8):
        b, h = c // 2, c % 2
        xc = np.zeros((33 * 128, D), np.float32)
        src = x[b, h * TOK:]
        xc[:src.shape[0] if src.shape[0] <= 33 * 128 else 33 * 128] = src[:33 * 128]
        m = dict(common)
        m["x"] = xc
        in_maps.append(m)
    return in_maps


def kernel(**inputs):
    in_maps = _host_inputs(**inputs)
    nc = build_nc(dbg=0)
    res = run_bass_kernel_spmd(nc, in_maps, core_ids=list(range(8)))
    out = np.empty((4, 4096, D), np.float32)
    for c in range(8):
        b, h = c // 2, c % 2
        out[b, h * TOK:(h + 1) * TOK] = res.results[c]["out"]
    return out
